# version 03 (final)
# baseline (speedup 1.0000x reference)
import numpy as np
import ml_dtypes
from contextlib import ExitStack

import concourse.bass as bass
import concourse.mybir as mybir
from concourse.bass_utils import run_bass_kernel_spmd

F32 = mybir.dt.float32
BF16 = mybir.dt.bfloat16
AF = mybir.ActivationFunctionType
ALU = mybir.AluOpType
AX = mybir.AxisListType

B, S, HID = 2, 4096, 4096
H, P, G = 128, 64, 8
INTER = 8192
PROJ = 18560
CHK = 128
EPS = 1e-5
TOWN = 1024
NCH = 8
PREF = 3072
NPW = 3
NKC = 32
NCB = 145
NCORE = 8

ENGS = ("te", "dv", "ac", "gp", "sy", "sc")
SEM2ENG = {"sTE": "te", "sDV": "dv", "sAC": "ac", "sW": "gp", "sW0": "gp", "sW1": "gp", "sWC": "gp",
           "sS": "sy", "sX": "sy", "sT1": "sy", "sC0": "sy", "sC1": "sy", "sO": "sy", "sN": "sy",
           "sR": "sc", "sB": "sy"}
TRACE = None
LIMIT = None
SUBLIM = None
CHB_LIM = None


class Emitter:
    def __init__(self, sems):
        self.sems = sems
        self.cnt = {k: 0 for k in sems}
        self.q = {e: [] for e in ENGS}

    def newphase(self):
        self.q = {e: [] for e in ENGS}

    def put(self, eng, fn):
        if eng == "sc":
            eng = "sy"
        self.q[eng].append(fn)

    def wait(self, eng, sem, val):
        if eng == "sc":
            eng = "sy"
        if val > 0:
            if TRACE is not None:
                TRACE[eng].append(("wait", sem, val))
            h = self.sems[sem]
            self.q[eng].append(lambda e, h=h, v=val: e.wait_ge(h, v))

    def inc(self, sem, n=1):
        self.cnt[sem] += n
        if TRACE is not None:
            TRACE[SEM2ENG[sem]].append(("inc", sem, self.cnt[sem]))
        return self.cnt[sem]

    def inc_last(self, eng, sem, n=1):
        h = self.sems[sem]
        f = self.q[eng].pop()
        self.q[eng].append(lambda e, f=f, h=h, n=n: f(e).then_inc(h, n))
        v = self.inc(sem, n)
        self.q[eng].append(lambda e, h=h, v=v: e.wait_ge(h, v))
        if TRACE is not None:
            TRACE[eng].append(("wait", sem, v))
        return v

    def fence(self, eng):
        sem = {"te": "sTE", "dv": "sDV", "ac": "sAC"}[eng]
        return self.inc_last(eng, sem)

    def run(self, eng, e):
        for fn in self.q[eng]:
            fn(e)


def psl(pt, off, rows, cols):
    return pt[0:rows, off:off + cols]


def TRP(e, out_ap, in_ap, ident):
    return e.matmul(out_ap, in_ap, ident, start=True, stop=True)


def build_nc():
    nc = bass.Bass("TRN2", target_bir_lowering=False)

    xo = nc.dram_tensor("xo", [TOWN, HID], F32, kind="ExternalInput").ap()
    xp = nc.dram_tensor("xp", [PREF, HID], F32, kind="ExternalInput").ap()
    mskI = nc.dram_tensor("mskI", [NPW, 128, TOWN], F32, kind="ExternalInput").ap()
    WinT = nc.dram_tensor("WinT", [NCB, 128, 4096], F32, kind="ExternalInput").ap()
    WoT = nc.dram_tensor("WoT", [8, 128, 32768], F32, kind="ExternalInput").ap()
    WbTt = nc.dram_tensor("WbTt", [8, 128, 16384], F32, kind="ExternalInput").ap()
    FW2 = nc.dram_tensor("FW2", [128, 4352], F32, kind="ExternalInput").ap()
    BlT = nc.dram_tensor("BlT", [128, 4096], F32, kind="ExternalInput").ap()
    cwI = nc.dram_tensor("cwI", [128, 320], F32, kind="ExternalInput").ap()
    cbI = nc.dram_tensor("cbI", [128, 80], F32, kind="ExternalInput").ap()
    nwI = nc.dram_tensor("nwI", [128, 64], F32, kind="ExternalInput").ap()
    dtbI = nc.dram_tensor("dtbI", [128, 1], F32, kind="ExternalInput").ap()
    alogI = nc.dram_tensor("alogI", [128, 1], F32, kind="ExternalInput").ap()
    dbcI = nc.dram_tensor("dbcI", [128, 128], F32, kind="ExternalInput").ap()
    bbI = nc.dram_tensor("bbI", [128, 4096], F32, kind="ExternalInput").ap()
    out = nc.dram_tensor("out", [TOWN, HID], F32, kind="ExternalOutput").ap()

    sp_hstok = nc.dram_tensor("sp_hstok", [NCH, 128, INTER], BF16, kind="Internal").ap()
    sp_hsf = nc.dram_tensor("sp_hsf", [NCH, 128, 8192], BF16, kind="Internal").ap()
    sp_gate = nc.dram_tensor("sp_gate", [NCH, 128, 8192], BF16, kind="Internal").ap()
    sp_bc = nc.dram_tensor("sp_bc", [NCH, 128, 2048], BF16, kind="Internal").ap()
    sp_norm = nc.dram_tensor("sp_norm", [NCH, 128, 8192], BF16, kind="Internal").ap()
    sp_xnt = nc.dram_tensor("sp_xnt", [NCH, 128, 4096], BF16, kind="Internal").ap()

    id_bf_c = nc.inline_tensor(np.eye(128, dtype=ml_dtypes.bfloat16), "id_bf_c")
    id_f_c = nc.inline_tensor(np.eye(128, dtype=np.float32), "id_f_c")
    tri_c = nc.inline_tensor(np.triu(np.ones((128, 128), dtype=np.float32)), "tri_c")

    es = ExitStack()

    def sbt(name, shape, dt):
        return es.enter_context(nc.sbuf_tensor(name, shape, dt))

    idb = sbt("idb", [128, 128], BF16)
    idf = sbt("idf", [128, 128], F32)
    tri = sbt("tri", [128, 128], F32)
    zer = sbt("zer", [128, 128], F32)
    onec = sbt("onec", [128, 32], BF16)
    cw = sbt("cw", [128, 320], F32)
    cbv = sbt("cbv", [128, 80], F32)
    nw = sbt("nw", [128, 64], F32)
    dtb = sbt("dtb", [128, 1], F32)
    acol = sbt("acol", [128, 1], F32)
    dbc = sbt("dbc", [128, 128], F32)
    fw2 = sbt("fw2", [128, 32 * 136], BF16)
    tails = sbt("tails", [128, 243], F32)
    Rst = sbt("Rst", [128, H * P], F32)
    hwT = sbt("hwT", [128, NCH * 128], BF16)
    dt_f = sbt("dt_f", [128, TOWN], F32)
    acum = sbt("acum", [128, TOWN], F32)
    ah = sbt("ah", [128, TOWN], BF16)
    al = sbt("al", [128, TOWN], BF16)
    dtT = sbt("dtT", [128, 128], F32)
    wxf = sbt("wxf", [128, 128], F32)
    wxT = sbt("wxT", [128, 128], F32)
    acT = sbt("acT", [128, 128], F32)
    eAT = sbt("eAT", [128, 128], F32)
    dsf = sbt("dsf", [128, 128], F32)
    CDt = sbt("CDt", [128, 128], F32)
    hls = sbt("hls", [128, 128], BF16)
    lgs = sbt("lgs", [128, 8], F32)
    lgt = sbt("lgt", [128, 8], F32)
    lgt2 = sbt("lgt2", [128, 8], F32)
    rt1 = sbt("rt1", [128, 1], F32)
    rt2 = sbt("rt2", [128, 1], F32)
    rt3 = sbt("rt3", [128, 1], F32)
    rt4 = sbt("rt4", [128, 1], F32)
    rt5 = sbt("rt5", [128, 1], F32)
    rt6 = sbt("rt6", [128, 1], F32)
    jnk = sbt("jnk", [128, 1], F32)
    rtb = sbt("rtb", [128, 8], BF16)
    hwb = sbt("hwb", [128, 128], BF16)

    ps = [es.enter_context(nc.psum_tensor(f"ps{i}", [128, 512], F32)) for i in range(8)]

    sems = {}
    for s in ("sW", "sW0", "sW1", "sWC", "sS", "sX", "sT1", "sC0", "sC1", "sO", "sN", "sR", "sTE", "sDV", "sAC", "sB"):
        sems[s] = es.enter_context(nc.semaphore(s))
    em = Emitter(sems)
    (SW, SW0, SW1, SWC, SS, SX, ST1, SC0, SC1, SO, SN, SR, STE, SDV, SAC, SB) = (
        sems[k] for k in ("sW", "sW0", "sW1", "sWC", "sS", "sX", "sT1", "sC0", "sC1", "sO", "sN",
                          "sR", "sTE", "sDV", "sAC", "sB"))
    SWP = [SW0, SW1]
    SCP = [SC0, SC1]

    def selb(h):
        return bass.AP(idb, h, [[128, 128], [0, 128]])

    def self32(h):
        return bass.AP(idf, h, [[128, 128], [0, 128]])

    st = {"ps7": 0, "ps6": 0, "ps45": 0, "ps2dv": 0, "ps2ac": 0, "ps01": 0,
          "ps3": 0, "dt_dv": 0, "acc_free": 0, "cvo_free": [(None, 0), (None, 0)],
          "trT_free": 0, "mask_free": 0, "chunk_end": {}, "y_sb": None}

    def flush_block():
        with nc.Block() as blk:
            @blk.tensor
            def _(e):
                em.run("te", e)

            @blk.vector
            def _(e):
                em.run("dv", e)

            @blk.scalar
            def _(e):
                em.run("ac", e)

            @blk.gpsimd
            def _(e):
                em.run("gp", e)

            @blk.sync
            def _(e):
                em.run("sy", e)
        em.newphase()

    def phase_barrier():
        for sem in ("sS", "sX", "sT1", "sC0", "sC1", "sO", "sN"):
            em.wait("sy", sem, em.cnt[sem])
        em.put("sy", lambda e: e.nop().then_inc(SB, 1))
        em.inc("sB")
        em.wait("sc", "sR", em.cnt["sR"])
        em.put("sc", lambda e: e.nop().then_inc(SB, 1))
        em.inc("sB")
        for sem in ("sW", "sW0", "sW1", "sWC"):
            em.wait("gp", sem, em.cnt[sem])
        em.put("gp", lambda e: e.nop().then_inc(SB, 1))
        em.inc("sB")
        b = em.cnt["sB"]
        for eng in ("te", "dv", "ac", "gp", "sy", "sc"):
            em.wait(eng, "sB", b)
            em.wait(eng, "sTE", em.cnt["sTE"])
            em.wait(eng, "sDV", em.cnt["sDV"])
            em.wait(eng, "sAC", em.cnt["sAC"])

    for (dst, src) in ((idb, id_bf_c.ap()), (idf, id_f_c.ap()), (tri, tri_c.ap()),
                       (cw, cwI), (cbv, cbI), (nw, nwI), (dtb, dtbI),
                       (dbc, dbcI), (acol, alogI)):
        em.put("sy", lambda e, d=dst, s=src: e.dma_start(out=d[:], in_=s).then_inc(SS, 16))
        em.inc("sS", 16)
    s_init = em.cnt["sS"]
    em.put("gp", lambda e: e.dma_start(out=fw2[:], in_=FW2).then_inc(SW0, 16))
    st_fw2 = em.inc("sW0", 16)
    em.put("dv", lambda e: e.memset(zer[:], 0.0))
    em.put("dv", lambda e: e.memset(tails[:], 0.0))
    em.put("dv", lambda e: e.memset(Rst[:], 0.0))
    em.put("dv", lambda e: e.memset(onec[:], 1.0))
    em.wait("ac", "sS", s_init)
    em.put("ac", lambda e: e.activation(rt1[:], acol[:], AF.Exp, bias=0.0, scale=1.0))
    em.inc_last("ac", "sAC")
    em.wait("dv", "sAC", em.cnt["sAC"])
    em.put("dv", lambda e: e.tensor_scalar(acol[:], rt1[:], -1.0, None, ALU.mult))
    em.inc_last("dv", "sDV")
    flush_block()

    def phase_a(w, own, xa, xq, xnb, xnT, wchb, ext, acc1, acc2, cvo, trT, mskb):
        if SUBLIM == "none":
            return
        xsrc = xo if own else xp
        chbs = list(range(NCB)) if own else list(range(64, NCB))
        if not own:
            em.wait("sc", "sDV", st["mask_free"])
            em.put("sc", lambda e, w=w: e.dma_start(out=mskb[:], in_=mskI[w]).then_inc(SR, 16))
            em.inc("sR", 16)
        msk_r = em.cnt["sR"]
        prev_xa_done = 0
        for t in range(NCH):
            r0 = (w * TOWN if not own else 0) + t * CHK
            em.wait("sy", "sDV", prev_xa_done)
            em.put("sy", lambda e, r0=r0, x=xsrc: e.dma_start(out=xa[:], in_=x[r0:r0 + 128, :]).then_inc(SX, 16))
            sS_t = em.inc("sX", 16)
            em.wait("dv", "sX", sS_t)
            em.put("dv", lambda e: e.tensor_reduce(rt1[:], xa[:], AX.X, ALU.add))
            em.put("dv", lambda e: e.tensor_tensor(xq[:], xa[:, 0:2048], xa[:, 0:2048], ALU.mult))
            em.fence("dv")
            em.put("dv", lambda e: e.tensor_reduce(rt4[:], xq[:], AX.X, ALU.add))
            em.fence("dv")
            em.put("dv", lambda e: e.tensor_tensor(xq[:], xa[:, 2048:4096], xa[:, 2048:4096], ALU.mult))
            em.fence("dv")
            em.put("dv", lambda e: e.tensor_reduce(rt5[:], xq[:], AX.X, ALU.add))
            em.fence("dv")
            em.put("dv", lambda e: e.tensor_scalar(rt3[:], rt1[:], 1.0 / 4096, None, ALU.mult))
            em.fence("dv")
            em.put("dv", lambda e: e.tensor_tensor(rt6[:], rt3[:], rt3[:], ALU.mult))
            em.put("dv", lambda e: e.tensor_tensor(rt2[:], rt4[:], rt5[:], ALU.add))
            em.fence("dv")
            em.put("dv", lambda e: e.scalar_tensor_tensor(rt2[:], rt2[:], 1.0 / 4096, rt6[:], ALU.mult, ALU.subtract))
            em.fence("dv")
            em.put("dv", lambda e: e.tensor_scalar(rt2[:], rt2[:], 1.0, EPS, ALU.mult, ALU.add))
            em.fence("dv")
            em.put("dv", lambda e: e.reciprocal(rt2[:], rt2[:]))
            dvm = em.inc_last("dv", "sDV")
            em.wait("ac", "sDV", dvm)
            em.put("ac", lambda e: e.activation(rt2[:], rt2[:], AF.Sqrt, bias=0.0, scale=1.0))
            acm = em.inc_last("ac", "sAC")
            em.wait("dv", "sAC", acm)
            em.put("dv", lambda e: e.tensor_scalar(rt3[:], rt1[:], -1.0 / 4096, None, ALU.mult))
            em.fence("dv")
            em.put("dv", lambda e: e.tensor_scalar(xq[:, 0:1024], xa[:, 0:1024], rt3[:, 0:1], None, ALU.add))
            em.put("dv", lambda e: e.tensor_scalar(xq[:, 1024:2048], xa[:, 1024:2048], rt3[:, 0:1], None, ALU.add))
            em.fence("dv")
            em.put("dv", lambda e: e.tensor_scalar(xnb[:, 0:1024], xq[:, 0:1024], rt2[:, 0:1], None, ALU.mult))
            em.put("dv", lambda e: e.tensor_scalar(xnb[:, 1024:2048], xq[:, 1024:2048], rt2[:, 0:1], None, ALU.mult))
            em.fence("dv")
            em.put("dv", lambda e: e.tensor_scalar(xq[:, 0:1024], xa[:, 2048:3072], rt3[:, 0:1], None, ALU.add))
            em.put("dv", lambda e: e.tensor_scalar(xq[:, 1024:2048], xa[:, 3072:4096], rt3[:, 0:1], None, ALU.add))
            em.fence("dv")
            em.put("dv", lambda e: e.tensor_scalar(xnb[:, 2048:3072], xq[:, 0:1024], rt2[:, 0:1], None, ALU.mult))
            em.put("dv", lambda e: e.tensor_scalar(xnb[:, 3072:4096], xq[:, 1024:2048], rt2[:, 0:1], None, ALU.mult))
            dvn = em.inc_last("dv", "sDV")
            prev_xa_done = dvn
            em.wait("te", "sDV", dvn)
            for r in range(4):
                em.wait("te", "sDV", st["ps7"])
                for j in range(8):
                    kc = r * 8 + j
                    em.put("te", lambda e, kc=kc, j=j: TRP(e, 
                        psl(ps[7], j * 128, 128, 128) if j < 4 else psl(ps[6], (j - 4) * 128, 128, 128),
                        xnb[:, kc * 128:(kc + 1) * 128], idb[:]))
                tem = em.inc_last("te", "sTE")
                em.wait("dv", "sTE", tem)
                for j in range(8):
                    kc = r * 8 + j
                    pt = ps[7] if j < 4 else ps[6]
                    off = (j % 4) * 128
                    em.put("dv", lambda e, kc=kc, t=t, pt=pt, off=off: e.tensor_copy(
                        xnT[:, kc * 1024 + t * 128: kc * 1024 + (t + 1) * 128],
                        psl(pt, off, 128, 128)))
                st["ps7"] = em.inc_last("dv", "sDV")
                st["ps6"] = st["ps7"]
        xnt_ready = em.cnt["sDV"]
        if SUBLIM == "ln":
            return
        if own:
            em.wait("sy", "sDV", xnt_ready)
            for z in range(NCH):
                xv = bass.AP(xnT, z * 128, [[NKC * 1024, 128], [1024, NKC], [1, 128]])
                em.put("sy", lambda e, z=z, xv=xv: e.dma_start(out=sp_xnt[z], in_=xv).then_inc(SS, 16))
                em.inc("sS", 16)
        if SUBLIM == "spx":
            return
        if CHB_LIM:
            chbs = chbs[:CHB_LIM]
        reuse = {}
        te_chb = {}
        for i, cbi in enumerate(chbs):
            if i >= 2 and reuse.get(i - 2):
                em.wait("gp", *reuse[i - 2])
            em.put("gp", lambda e, cbi=cbi, i=i: e.dma_start(
                out=wchb[:, (i % 2) * 4096:(i % 2 + 1) * 4096],
                in_=WinT[cbi]).then_inc(SWP[i % 2], 16))
            w_i = em.inc(f"sW{i % 2}", 16)
            em.wait("te", f"sW{i % 2}", w_i)
            if i == 0:
                em.wait("te", "sDV", xnt_ready)
            if i >= 2 and reuse.get(i - 2):
                em.wait("te", *reuse[i - 2])
            pA = ps[0] if i % 2 == 0 else ps[2]
            pB = ps[1] if i % 2 == 0 else ps[3]
            for k in range(NKC):
                em.put("te", lambda e, k=k, i=i, pA=pA: e.matmul(
                    pA[:], wchb[:, (i % 2) * 4096 + k * 128:(i % 2) * 4096 + (k + 1) * 128],
                    xnT[:, k * 1024:k * 1024 + 512], start=(k == 0), stop=(k == NKC - 1)))
                em.put("te", lambda e, k=k, i=i, pB=pB: e.matmul(
                    pB[:], wchb[:, (i % 2) * 4096 + k * 128:(i % 2) * 4096 + (k + 1) * 128],
                    xnT[:, k * 1024 + 512:k * 1024 + 1024], start=(k == 0), stop=(k == NKC - 1)))
            te_chb[i] = em.inc_last("te", "sTE")
            slot = i % 2
            if cbi < 64:
                em.wait("ac", "sTE", te_chb[i])
                cs, cv = st["cvo_free"][slot]
                if cs:
                    em.wait("ac", cs, cv)
                em.put("ac", lambda e, pA=pA, slot=slot: e.activation(cvo[:, slot * 1024:slot * 1024 + 512], pA[:], AF.Silu, bias=0.0, scale=1.0))
                em.put("ac", lambda e, pB=pB, slot=slot: e.activation(cvo[:, slot * 1024 + 512:slot * 1024 + 1024], pB[:], AF.Silu, bias=0.0, scale=1.0))
                am = em.inc_last("ac", "sAC")
                reuse[i] = ("sAC", am)
                em.wait("sy", "sAC", am)
                em.put("sy", lambda e, cbi=cbi, slot=slot: e.dma_start(
                    out=sp_gate[:, :, cbi * 128:(cbi + 1) * 128].rearrange("z p b -> p z b"),
                    in_=cvo[:, slot * 1024:(slot + 1) * 1024].rearrange("p (z b) -> p z b", z=8)).then_inc(SCP[slot], 16))
                sm = em.inc(f"sC{slot}", 16)
                st["cvo_free"][slot] = (f"sC{slot}", sm)
            elif cbi < 144:
                hb = cbi - 64
                em.wait("dv", "sTE", te_chb[i])
                em.put("dv", lambda e, hb=hb: e.tensor_copy(ext[:, 0:3], tails[:, hb * 3:(hb + 1) * 3]))
                em.put("dv", lambda e, pA=pA: e.tensor_copy(ext[:, 3:515], pA[:]))
                em.put("dv", lambda e, pB=pB: e.tensor_copy(ext[:, 515:1027], pB[:]))
                dm0 = em.inc_last("dv", "sDV")
                reuse[i] = ("sDV", dm0)
                em.wait("dv", "sAC", st["acc_free"])
                em.put("dv", lambda e, hb=hb: e.tensor_scalar(acc1[:], ext[:, 0:1024], cw[:, hb * 4:hb * 4 + 1], None, ALU.mult))
                em.fence("dv")
                em.put("dv", lambda e, hb=hb: e.scalar_tensor_tensor(acc2[:], ext[:, 1:1025], cw[:, hb * 4 + 1:hb * 4 + 2], acc1[:], ALU.mult, ALU.add))
                em.fence("dv")
                em.put("dv", lambda e, hb=hb: e.scalar_tensor_tensor(acc1[:], ext[:, 2:1026], cw[:, hb * 4 + 2:hb * 4 + 3], acc2[:], ALU.mult, ALU.add))
                em.fence("dv")
                em.put("dv", lambda e, hb=hb: e.scalar_tensor_tensor(acc2[:], ext[:, 3:1027], cw[:, hb * 4 + 3:hb * 4 + 4], acc1[:], ALU.mult, ALU.add))
                em.put("dv", lambda e, hb=hb: e.tensor_copy(tails[:, hb * 3:(hb + 1) * 3], ext[:, 1024:1027]))
                dm = em.inc_last("dv", "sDV")
                em.wait("ac", "sDV", dm)
                cs, cv = st["cvo_free"][slot]
                if cs:
                    em.wait("ac", cs, cv)
                em.put("ac", lambda e, hb=hb, slot=slot: e.activation(cvo[:, slot * 1024:(slot + 1) * 1024], acc2[:], AF.Silu, bias=cbv[:, hb:hb + 1], scale=1.0))
                am = em.inc_last("ac", "sAC")
                st["acc_free"] = am
                if cbi < 128:
                    if own:
                        em.wait("sy", "sAC", am)
                        em.put("sy", lambda e, hb=hb, slot=slot: e.dma_start(
                            out=sp_hsf[:, :, hb * 128:(hb + 1) * 128].rearrange("z p b -> p z b"),
                            in_=cvo[:, slot * 1024:(slot + 1) * 1024].rearrange("p (z b) -> p z b", z=8)).then_inc(SCP[slot], 16))
                        em.inc(f"sC{slot}", 16)
                    em.wait("te", "sAC", am)
                    em.wait("te", "sDV", st["ps7"])
                    for z in range(8):
                        pt = ps[7] if z < 4 else ps[6]
                        em.put("te", lambda e, z=z, pt=pt, slot=slot: TRP(e, 
                            psl(pt, (z % 4) * 128, 128, 128),
                            cvo[:, slot * 1024 + z * 128:slot * 1024 + (z + 1) * 128], idb[:]))
                    tm = em.inc_last("te", "sTE")
                    em.wait("dv", "sTE", tm)
                    em.wait("dv", "sT1", st["trT_free"])
                    for z in range(8):
                        pt = ps[7] if z < 4 else ps[6]
                        em.put("dv", lambda e, z=z, pt=pt: e.tensor_copy(
                            trT[:, z * 128:(z + 1) * 128], psl(pt, (z % 4) * 128, 128, 128)))
                    st["ps7"] = em.inc_last("dv", "sDV")
                    st["ps6"] = st["ps7"]
                    em.wait("sy", "sDV", st["ps7"])
                    em.put("sy", lambda e, hb=hb: e.dma_start(
                        out=sp_hstok[:, :, hb * 128:(hb + 1) * 128].rearrange("z p b -> p z b"),
                        in_=trT[:].rearrange("p (z b) -> p z b", z=8)).then_inc(ST1, 16))
                    st["trT_free"] = em.inc("sT1", 16)
                    st["cvo_free"][slot] = (f"sC{slot}", em.cnt[f"sC{slot}"]) if own else ("sTE", tm)
                else:
                    bslot = hb - 64
                    em.wait("sy", "sAC", am)
                    em.put("sy", lambda e, bslot=bslot, slot=slot: e.dma_start(
                        out=sp_bc[:, :, bslot * 128:(bslot + 1) * 128].rearrange("z p b -> p z b"),
                        in_=cvo[:, slot * 1024:(slot + 1) * 1024].rearrange("p (z b) -> p z b", z=8)).then_inc(SCP[slot], 16))
                    sm = em.inc(f"sC{slot}", 16)
                    st["cvo_free"][slot] = (f"sC{slot}", sm)
            else:
                em.wait("ac", "sTE", te_chb[i])
                em.wait("ac", "sDV", em.cnt["sDV"])
                if not own:
                    em.wait("ac", "sR", msk_r)
                em.put("ac", lambda e, pA=pA: e.activation(acc1[:, 0:512], pA[:], AF.Abs, bias=dtb[:, 0:1], scale=1.0))
                em.put("ac", lambda e, pB=pB: e.activation(acc1[:, 512:1024], pB[:], AF.Abs, bias=dtb[:, 0:1], scale=1.0))
                em.fence("ac")
                em.put("ac", lambda e: e.activation(acc1[:], acc1[:], AF.Exp, bias=0.0, scale=-1.0))
                em.fence("ac")
                em.put("ac", lambda e: e.activation(acc1[:], acc1[:], AF.Ln, bias=1.0, scale=1.0))
                em.put("ac", lambda e, pA=pA: e.activation(acc2[:, 0:512], pA[:], AF.Relu, bias=dtb[:, 0:1], scale=1.0))
                em.put("ac", lambda e, pB=pB: e.activation(acc2[:, 512:1024], pB[:], AF.Relu, bias=dtb[:, 0:1], scale=1.0))
                am = em.inc_last("ac", "sAC")
                reuse[i] = ("sAC", am)
                em.wait("dv", "sAC", am)
                em.put("dv", lambda e: e.tensor_tensor(dt_f[:], acc1[:], acc2[:], ALU.add))
                em.fence("dv")
                em.put("dv", lambda e: e.tensor_scalar(dt_f[:], dt_f[:], 100.0, None, ALU.min))
                em.fence("dv")
                if not own:
                    em.wait("dv", "sR", msk_r)
                    em.put("dv", lambda e: e.tensor_tensor(dt_f[:], dt_f[:], mskb[:], ALU.mult))
                    em.fence("dv")
                em.put("dv", lambda e: e.tensor_scalar(acc2[:], dt_f[:], acol[:, 0:1], None, ALU.mult))
                em.fence("dv")
                for z in range(8):
                    em.put("dv", lambda e, z=z: e.tensor_tensor_scan(
                        acum[:, z * 128:(z + 1) * 128], acc2[:, z * 128:(z + 1) * 128],
                        acc2[:, z * 128:(z + 1) * 128], 0.0, ALU.add, ALU.bypass))
                em.fence("dv")
                em.put("dv", lambda e: e.tensor_copy(ah[:], acum[:]))
                em.fence("dv")
                em.put("dv", lambda e: e.tensor_copy(acc1[:], ah[:]))
                em.fence("dv")
                em.put("dv", lambda e: e.tensor_tensor(al[:], acum[:], acc1[:], ALU.subtract))
                st["dt_dv"] = em.inc_last("dv", "sDV")
                st["mask_free"] = st["dt_dv"]

    def phase_b(w, own, hstok, hsf, gf, bcs, Btok, SMb, dcl, Eb, EAb, MTb, xch, xds, msr, rsb, g2q, xnk, RstB, ytmp, ytmp2, ytmp3, nrt):
        em.put("dv", lambda e: e.memset(rsb[:], 0.0))
        em.put("dv", lambda e: e.tensor_copy(RstB[:], Rst[:]))
        em.inc_last("dv", "sDV")
        for z in range(NCH):
            pe = st["chunk_end"].get("prev")
            if pe:
                for sem, v in pe:
                    em.wait("sc", sem, v)
            em.put("sc", lambda e, z=z: e.dma_start(out=hstok[:], in_=sp_hstok[z]).then_inc(SR, 16))
            em.inc("sR", 16)
            em.put("sc", lambda e, z=z: e.dma_start(out=bcs[:], in_=sp_bc[z]).then_inc(SR, 16))
            r_bc = em.inc("sR", 16)
            if own:
                em.put("sc", lambda e, z=z: e.dma_start(out=hsf[:], in_=sp_hsf[z]).then_inc(SR, 16))
                em.inc("sR", 16)
                em.put("sc", lambda e, z=z: e.dma_start(out=gf[:], in_=sp_gate[z]).then_inc(SR, 16))
                em.inc("sR", 16)
                em.put("sc", lambda e, z=z: e.dma_start(out=xnk[:], in_=sp_xnt[z]).then_inc(SR, 16))
                em.inc("sR", 16)
            r_all = em.cnt["sR"]
            zsl0, zsl1 = z * 128, (z + 1) * 128
            em.wait("ac", "sDV", max(st["dt_dv"], st.get("dsf_free", 0)))
            em.put("ac", lambda e, zsl0=zsl0, zsl1=zsl1: e.activation(
                dsf[:], acum[:, zsl0:zsl1], AF.Exp, bias=acum[:, zsl1 - 1:zsl1], scale=-1.0))
            am_ds = em.inc_last("ac", "sAC")
            em.wait("dv", "sAC", am_ds)
            em.put("dv", lambda e, zsl0=zsl0, zsl1=zsl1: e.tensor_tensor(
                wxf[:], dt_f[:, zsl0:zsl1], dsf[:], ALU.mult))
            dv_wx = em.inc_last("dv", "sDV")
            st["dsf_free"] = dv_wx
            em.wait("te", "sDV", max(dv_wx, st["ps7"]))
            em.wait("te", "sR", r_all)
            for g in range(8):
                pt = ps[7] if g < 4 else ps[6]
                em.put("te", lambda e, g=g, pt=pt: TRP(e, 
                    psl(pt, (g % 4) * 128, 128, 128), bcs[:, g * 128:(g + 1) * 128], idb[:]))
            tm1 = em.inc_last("te", "sTE")
            em.wait("dv", "sTE", tm1)
            for g in range(8):
                pt = ps[7] if g < 4 else ps[6]
                em.put("dv", lambda e, g=g, pt=pt: e.tensor_copy(
                    Btok[:, g * 128:(g + 1) * 128], psl(pt, (g % 4) * 128, 128, 128)))
            st["ps7"] = em.inc_last("dv", "sDV")
            st["ps6"] = st["ps7"]
            em.wait("te", "sDV", st["ps7"])
            if own:
                em.put("te", lambda e, zsl0=zsl0, zsl1=zsl1: TRP(e, 
                    psl(ps[7], 0, 128, 128), dt_f[:, zsl0:zsl1], idf[:]))
            em.put("te", lambda e: TRP(e, psl(ps[7], 128, 128, 128), wxf[:], idf[:]))
            em.put("te", lambda e, zsl0=zsl0, zsl1=zsl1: TRP(e, 
                psl(ps[7], 256, 128, 128), acum[:, zsl0:zsl1], idf[:]))
            tm2 = em.inc_last("te", "sTE")
            em.wait("dv", "sTE", tm2)
            if own:
                em.put("dv", lambda e: e.tensor_copy(dtT[:], psl(ps[7], 0, 128, 128)))
            em.put("dv", lambda e: e.tensor_copy(wxT[:], psl(ps[7], 128, 128, 128)))
            em.put("dv", lambda e: e.tensor_copy(acT[:], psl(ps[7], 256, 128, 128)))
            st["ps7"] = em.inc_last("dv", "sDV")
            dv_tr = st["ps7"]
            em.wait("ac", "sDV", dv_tr)
            em.put("ac", lambda e: e.activation(eAT[:], acT[:], AF.Exp, bias=0.0, scale=1.0))
            am_eat = em.inc_last("ac", "sAC")
            em.wait("te", "sAC", am_eat)
            em.wait("te", "sDV", st["ps6"])
            em.put("te", lambda e: e.matmul(psl(ps[6], 0, 128, 128), self32(127), eAT[:], start=True, stop=True))
            tm3 = em.inc_last("te", "sTE")
            em.wait("dv", "sTE", tm3)
            em.put("dv", lambda e: e.tensor_copy(CDt[:], psl(ps[6], 0, 128, 128)))
            st["ps6"] = em.inc_last("dv", "sDV")
            if own:
                em.wait("te", "sDV", st["ps45"])
                for g in range(8):
                    pt, off = (ps[4], (g % 4) * 128) if g < 4 else (ps[5], (g % 4) * 128)
                    em.put("te", lambda e, g=g, pt=pt, off=off: e.matmul(
                        psl(pt, off, 128, 128), bcs[:, g * 128:(g + 1) * 128],
                        bcs[:, 1024 + g * 128:1024 + (g + 1) * 128], start=True, stop=True))
                tm4 = em.inc_last("te", "sTE")
                em.wait("dv", "sTE", tm4)
                for g in range(8):
                    pt, off = (ps[4], (g % 4) * 128) if g < 4 else (ps[5], (g % 4) * 128)
                    em.put("dv", lambda e, g=g, pt=pt, off=off: e.tensor_tensor(
                        SMb[:, g * 128:(g + 1) * 128], psl(pt, off, 128, 128), tri[:], ALU.mult))
                st["ps45"] = em.inc_last("dv", "sDV")
                em.wait("dv", "sR", r_all)
            em.wait("dv", "sN", st.get("norm_spill", 0))
            for bi in range(32):
                g = bi // 4
                h0 = bi * 4
                if own:
                    em.wait("te", "sDV", st["ps2dv"])
                    em.wait("te", "sAC", st["ps2ac"])
                    for j in range(4):
                        h = h0 + j
                        em.put("te", lambda e, h=h, j=j, zsl0=zsl0, zsl1=zsl1: e.matmul(
                            psl(ps[2], j * 128, 128, 128), selb(h), ah[:, zsl0:zsl1], start=True, stop=False))
                        em.put("te", lambda e, h=h, j=j, zsl0=zsl0, zsl1=zsl1: e.matmul(
                            psl(ps[2], j * 128, 128, 128), selb(h), al[:, zsl0:zsl1], start=False, stop=True))
                    tm_rb = em.inc_last("te", "sTE")
                    em.wait("dv", "sTE", tm_rb)
                    for j in range(4):
                        h = h0 + j
                        em.put("dv", lambda e, h=h, j=j: e.scalar_tensor_tensor(
                            dcl[:, j * 128:(j + 1) * 128], psl(ps[2], j * 128, 128, 128),
                            acT[:, h:h + 1], zer[:], ALU.subtract, ALU.min))
                        em.put("dv", lambda e, h=h, j=j: e.tensor_scalar(
                            xch[:, j * 64:(j + 1) * 64], hstok[:, h * 64:(h + 1) * 64],
                            dtT[:, h:h + 1], None, ALU.mult))
                        em.put("dv", lambda e, h=h, j=j: e.tensor_scalar(
                            xds[:, j * 64:(j + 1) * 64], hstok[:, h * 64:(h + 1) * 64],
                            wxT[:, h:h + 1], None, ALU.mult))
                    dv_b = em.inc_last("dv", "sDV")
                    st["ps2dv"] = dv_b
                    em.wait("ac", "sDV", dv_b)
                    em.put("ac", lambda e: e.activation(Eb[:], dcl[:], AF.Exp, bias=0.0, scale=1.0))
                    em.put("ac", lambda e: e.activation(EAb[0:64, :], ps[2][0:64, :], AF.Exp, bias=0.0, scale=1.0))
                    ac_b = em.inc_last("ac", "sAC")
                    st["ps2ac"] = ac_b
                    em.wait("dv", "sAC", ac_b)
                    for j in range(4):
                        em.put("dv", lambda e, j=j, g=g: e.tensor_tensor(
                            MTb[:, j * 128:(j + 1) * 128], Eb[:, j * 128:(j + 1) * 128],
                            SMb[:, g * 128:(g + 1) * 128], ALU.mult))
                    dv_mt = em.inc_last("dv", "sDV")
                    em.wait("te", "sDV", max(dv_mt, st["ps01"]))
                    for j in range(4):
                        h = h0 + j
                        em.put("te", lambda e, h=h, j=j: e.matmul(
                            psl(ps[0], j * 128, 64, 128), xch[:, j * 64:(j + 1) * 64],
                            MTb[:, j * 128:(j + 1) * 128], start=True, stop=True))
                        em.put("te", lambda e, h=h, j=j, g=g: e.matmul(
                            psl(ps[1], j * 128, 64, 128), RstB[:, h * 64:(h + 1) * 64],
                            bcs[:, 1024 + g * 128:1024 + (g + 1) * 128], start=True, stop=True))
                else:
                    em.wait("dv", "sR", r_all)
                    em.wait("dv", "sDV", dv_tr)
                    for j in range(4):
                        h = h0 + j
                        em.put("dv", lambda e, h=h, j=j: e.tensor_scalar(
                            xds[:, j * 64:(j + 1) * 64], hstok[:, h * 64:(h + 1) * 64],
                            wxT[:, h:h + 1], None, ALU.mult))
                    dv_b = em.inc_last("dv", "sDV")
                    em.wait("te", "sDV", dv_b)
                em.wait("te", "sDV", st["ps3"])
                for j in range(4):
                    em.put("te", lambda e, j=j, g=g: e.matmul(
                        psl(ps[3], j * 128, 128, 64), Btok[:, g * 128:(g + 1) * 128],
                        xds[:, j * 64:(j + 1) * 64], start=True, stop=True))
                tm_y = em.inc_last("te", "sTE")
                em.wait("dv", "sTE", tm_y)
                if own:
                    for j in range(4):
                        h = h0 + j
                        em.put("dv", lambda e, j=j: e.tensor_tensor(
                            ytmp[0:64, j * 128:(j + 1) * 128], psl(ps[1], j * 128, 64, 128),
                            EAb[0:64, j * 128:(j + 1) * 128], ALU.mult))
                        em.put("dv", lambda e, h=h, j=j: e.tensor_scalar(
                            ytmp3[0:64, j * 128:(j + 1) * 128],
                            hsf[(h % 2) * 64:(h % 2) * 64 + 64, (h // 2) * 128:(h // 2 + 1) * 128],
                            dbc[(h % 2) * 64:(h % 2) * 64 + 64, h:h + 1], None, ALU.mult))
                    em.fence("dv")
                    for j in range(4):
                        em.put("dv", lambda e, j=j: e.tensor_tensor(
                            ytmp2[0:64, j * 128:(j + 1) * 128], ytmp[0:64, j * 128:(j + 1) * 128],
                            psl(ps[0], j * 128, 64, 128), ALU.add))
                    em.fence("dv")
                    for j in range(4):
                        h = h0 + j
                        em.put("dv", lambda e, h=h, j=j: e.tensor_tensor(
                            st["y_sb"][(h % 2) * 64:(h % 2) * 64 + 64, (h // 2) * 128:(h // 2 + 1) * 128],
                            ytmp3[0:64, j * 128:(j + 1) * 128], ytmp2[0:64, j * 128:(j + 1) * 128], ALU.add))
                for j in range(4):
                    h = h0 + j
                    em.put("dv", lambda e, h=h, j=j: e.scalar_tensor_tensor(
                        Rst[:, h * 64:(h + 1) * 64], Rst[:, h * 64:(h + 1) * 64],
                        CDt[:, h:h + 1], psl(ps[3], j * 128, 128, 64), ALU.mult, ALU.add))
                if own:
                    em.fence("dv")
                    for j in range(4):
                        h = h0 + j
                        if not (z == NCH - 1 and bi == 31):
                            em.put("dv", lambda e, h=h: e.tensor_copy(
                                RstB[:, h * 64:(h + 1) * 64], Rst[:, h * 64:(h + 1) * 64]))
                dvy = em.inc_last("dv", "sDV")
                st["ps01"] = dvy
                st["ps3"] = dvy
            heads_done = em.cnt["sDV"]
            if own:
                y_sb = st["y_sb"]
                em.wait("ac", "sDV", heads_done)
                em.wait("ac", "sR", r_all)
                for g in range(8):
                    em.wait("ac", "sDV", st.get("g2q_dv", 0))
                    em.wait("ac", "sTE", st.get("g2q_te", 0))
                    em.put("ac", lambda e, g=g: e.activation(g2q[:], gf[:, g * 1024:(g + 1) * 1024], AF.Copy, bias=0.0, scale=1.0))
                    amg = em.inc_last("ac", "sAC")
                    em.wait("dv", "sAC", amg)
                    em.put("dv", lambda e, g=g: e.tensor_tensor(y_sb[:, g * 1024:(g + 1) * 1024], y_sb[:, g * 1024:(g + 1) * 1024], g2q[:], ALU.mult))
                    em.fence("dv")
                    em.put("dv", lambda e, g=g: e.tensor_tensor(g2q[:], y_sb[:, g * 1024:(g + 1) * 1024], y_sb[:, g * 1024:(g + 1) * 1024], ALU.mult))
                    dvg = em.inc_last("dv", "sDV")
                    st["g2q_dv"] = dvg
                    em.wait("te", "sDV", max(dvg, st["ps45"] if g == 0 else 0))
                    ptg = ps[4] if g < 4 else ps[5]
                    offg = (g % 4) * 128
                    for cc in range(8):
                        em.put("te", lambda e, cc=cc, ptg=ptg, offg=offg: e.matmul(
                            psl(ptg, offg, 32, 128), onec[:], g2q[:, cc * 128:(cc + 1) * 128],
                            start=(cc == 0), stop=(cc == 7)))
                    tmg = em.inc_last("te", "sTE")
                    st["g2q_te"] = tmg
                    em.wait("dv", "sTE", tmg)
                    em.put("dv", lambda e, g=g, ptg=ptg, offg=offg: e.tensor_copy(msr[0:1, g * 128:(g + 1) * 128], psl(ptg, offg, 1, 128)))
                em.fence("dv")
                em.put("dv", lambda e: e.tensor_scalar(msr[0:1, :], msr[0:1, :], 1.0 / 1024, EPS, ALU.mult, ALU.add))
                em.fence("dv")
                em.put("dv", lambda e: e.reciprocal(msr[0:1, :], msr[0:1, :]))
                dvr = em.inc_last("dv", "sDV")
                em.wait("ac", "sDV", dvr)
                em.put("ac", lambda e: e.activation(msr[0:1, :], msr[0:1, :], AF.Sqrt, bias=0.0, scale=1.0))
                amr = em.inc_last("ac", "sAC")
                em.wait("dv", "sAC", amr)
                em.put("dv", lambda e: e.tensor_copy(rsb[0:1, :], msr[0:1, :]))
                dvrb = em.inc_last("dv", "sDV")
                em.wait("te", "sDV", max(dvrb, st["ps6"]))
                em.put("te", lambda e: e.matmul(ps[5][:], selb(0), rsb[:, 0:512], start=True, stop=True))
                em.put("te", lambda e: e.matmul(ps[6][:], selb(0), rsb[:, 512:1024], start=True, stop=True))
                tmr = em.inc_last("te", "sTE")
                em.wait("dv", "sTE", tmr)
                em.wait("ac", "sTE", tmr)
                for cb0 in range(0, 64, 4):
                    for q_ in range(4):
                        cbx = cb0 + q_
                        g = cbx // 8
                        pt = ps[5] if g < 4 else ps[6]
                        off = (g % 4) * 128
                        em.put("dv", lambda e, cbx=cbx, q_=q_, pt=pt, off=off: e.tensor_tensor(
                            nrt[:, q_ * 128:(q_ + 1) * 128], y_sb[:, cbx * 128:(cbx + 1) * 128],
                            psl(pt, off, 128, 128), ALU.mult))
                    dvn = em.inc_last("dv", "sDV")
                    em.wait("ac", "sDV", dvn)
                    for q_ in range(4):
                        cbx = cb0 + q_
                        em.put("ac", lambda e, cbx=cbx, q_=q_: e.activation(
                            y_sb[:, cbx * 128:(cbx + 1) * 128], nrt[:, q_ * 128:(q_ + 1) * 128], AF.Copy,
                            bias=0.0, scale=nw[:, cbx:cbx + 1]))
                    acn = em.inc_last("ac", "sAC")
                    em.wait("dv", "sAC", acn)
                norm_done_dv = em.cnt["sDV"]
                norm_done_ac = em.cnt["sAC"]
                st["ps45"] = norm_done_dv
                st["ps6"] = norm_done_dv
                em.wait("sy", "sAC", norm_done_ac)
                em.put("sy", lambda e, z=z: e.dma_start(out=sp_norm[z], in_=y_sb[:]).then_inc(SN, 16))
                norm_spill = em.inc("sN", 16)
                st["norm_spill"] = norm_spill
                em.wait("te", "sR", r_all)
                em.wait("te", "sW0", st_fw2)
                em.wait("te", "sDV", st["ps6"])
                for k2 in range(32):
                    em.put("te", lambda e, k2=k2: e.matmul(
                        psl(ps[6], 0, 128, 136), xnk[:, k2 * 128:(k2 + 1) * 128],
                        fw2[:, k2 * 136:(k2 + 1) * 136], start=(k2 == 0), stop=(k2 == 31)))
                tmf = em.inc_last("te", "sTE")
                em.wait("dv", "sTE", tmf)
                em.put("dv", lambda e: e.tensor_copy(hls[:], psl(ps[6], 0, 128, 128)))
                em.put("dv", lambda e: e.tensor_copy(lgs[:], psl(ps[6], 128, 128, 8)))
                em.fence("dv")
                em.put("dv", lambda e: e.tensor_reduce(rt1[:], lgs[:], AX.X, ALU.max))
                em.put("dv", lambda e: e.tensor_scalar(lgt[:], lgs[:], -1.0, None, ALU.mult))
                em.fence("dv")
                em.put("dv", lambda e: e.tensor_scalar(lgt[:], lgt[:], rt1[:, 0:1], None, ALU.add))
                dvf = em.inc_last("dv", "sDV")
                st["ps6"] = dvf
                em.wait("ac", "sDV", dvf)
                em.put("ac", lambda e: e.activation(lgt[:], lgt[:], AF.Sign, bias=0.0, scale=1.0))
                am1 = em.inc_last("ac", "sAC")
                em.wait("dv", "sAC", am1)
                em.put("dv", lambda e: e.tensor_scalar(lgt[:], lgt[:], -1.0, 1.0, ALU.mult, ALU.add))
                em.fence("dv")
                em.put("dv", lambda e: e.scalar_tensor_tensor(lgt2[:], lgt[:], -1e30, lgs[:], ALU.mult, ALU.add))
                em.fence("dv")
                em.put("dv", lambda e: e.tensor_reduce(rt2[:], lgt2[:], AX.X, ALU.max))
                em.fence("dv")
                em.put("dv", lambda e: e.tensor_scalar(lgt2[:], lgs[:], -1.0, None, ALU.mult))
                em.fence("dv")
                em.put("dv", lambda e: e.tensor_tensor(rt3[:], rt1[:], rt2[:], ALU.subtract))
                em.put("dv", lambda e: e.tensor_scalar(lgt2[:], lgt2[:], rt2[:, 0:1], None, ALU.add))
                em.fence("dv")
                em.put("dv", lambda e: e.tensor_scalar(rt1[:], rt1[:], -1.0, None, ALU.mult))
                dvf2 = em.inc_last("dv", "sDV")
                em.wait("ac", "sDV", dvf2)
                em.put("ac", lambda e: e.activation(rt3[:], rt3[:], AF.Sigmoid, bias=0.0, scale=1.0))
                em.put("ac", lambda e: e.activation(lgt2[:], lgt2[:], AF.Sign, bias=0.0, scale=1.0))
                em.put("ac", lambda e: e.activation(lgs[:], lgs[:], AF.Exp, bias=rt1[:, 0:1], scale=1.0))
                am2 = em.inc_last("ac", "sAC")
                em.wait("dv", "sAC", am2)
                em.put("dv", lambda e: e.tensor_scalar(lgt2[:], lgt2[:], 0.0, None, ALU.max))
                em.fence("dv")
                em.put("dv", lambda e: e.tensor_scalar(lgt2[:], lgt2[:], -1.0, 1.0, ALU.mult, ALU.add))
                em.fence("dv")
                em.put("dv", lambda e: e.scalar_tensor_tensor(lgs[:], lgs[:], rt3[:, 0:1], lgt2[:], ALU.mult, ALU.mult))
                em.fence("dv")
                em.put("dv", lambda e: e.tensor_copy(rtb[:], lgs[:]))
                em.fence("dv")
                em.put("dv", lambda e: e.tensor_tensor(
                    hwb[:].rearrange("p (a b) -> p a b", a=8), hls[:].rearrange("p (a b) -> p a b", a=8),
                    bass.AP(rtb, 0, [[8, 128], [1, 8], [0, 16]]), ALU.mult))
                dvf3 = em.inc_last("dv", "sDV")
                em.wait("te", "sDV", max(dvf3, st["ps7"]))
                em.put("te", lambda e: TRP(e, psl(ps[7], 0, 128, 128), hwb[:], idb[:]))
                tmh = em.inc_last("te", "sTE")
                em.wait("dv", "sTE", tmh)
                em.put("dv", lambda e, z=z: e.tensor_copy(hwT[:, z * 128:(z + 1) * 128], psl(ps[7], 0, 128, 128)))
                st["ps7"] = em.inc_last("dv", "sDV")
                st["chunk_end"]["prev"] = [("sDV", em.cnt["sDV"]), ("sTE", em.cnt["sTE"]), ("sAC", em.cnt["sAC"])]
            else:
                st["chunk_end"]["prev"] = [("sDV", em.cnt["sDV"]), ("sTE", em.cnt["sTE"])]

    def phase_c(nhalf, xnth, wq, s_sb, gf32, xcs, blt, bbs):
        st["psS"] = em.cnt["sDV"]
        st["psF"] = em.cnt["sDV"]
        st["wq0"] = 0
        st["wq1"] = 0
        st["bbte"] = 0
        half_te_done = 0
        for half in range(2):
            em.wait("sc", "sTE", half_te_done)
            for zi in range(4):
                em.put("sc", lambda e, zi=zi, half=half: e.dma_start(
                    out=nhalf[:, zi * 8192:(zi + 1) * 8192], in_=sp_norm[half * 4 + zi]).then_inc(SR, 16))
                em.inc("sR", 16)
                em.put("sc", lambda e, zi=zi, half=half: e.dma_start(
                    out=xnth[:, zi * 4096:(zi + 1) * 4096], in_=sp_xnt[half * 4 + zi]).then_inc(SR, 16))
                em.inc("sR", 16)
            r_h = em.cnt["sR"]
            for ob in range(8):
                em.wait("gp", "sTE", st["bbte"])
                em.put("gp", lambda e, ob=ob: e.dma_start(out=blt[:], in_=BlT[:, ob * 512:(ob + 1) * 512]).then_inc(SWC, 16))
                em.inc("sWC", 16)
                em.put("gp", lambda e, ob=ob: e.dma_start(out=bbs[:], in_=bbI[:, ob * 512:(ob + 1) * 512]).then_inc(SWC, 16))
                w_cc = em.inc("sWC", 16)
                for kq in range(8):
                    em.wait("gp", "sTE", st[f"wq{kq % 2}"])
                    em.put("gp", lambda e, ob=ob, kq=kq: e.dma_start(
                        out=wq[:, (kq % 2) * 4096:(kq % 2 + 1) * 4096],
                        in_=WoT[ob][:, kq * 4096:(kq + 1) * 4096]).then_inc(SWP[kq % 2], 16))
                    wm = em.inc(f"sW{kq % 2}", 16)
                    em.wait("te", f"sW{kq % 2}", wm)
                    if kq == 0:
                        em.wait("te", "sR", r_h)
                        em.wait("te", "sDV", st["psS"])
                    for k in range(8):
                        kk = kq * 8 + k
                        for tl in range(4):
                            em.put("te", lambda e, kk=kk, k=k, kq=kq, tl=tl: e.matmul(
                                ps[tl][:],
                                nhalf[:, tl * 8192 + kk * 128:tl * 8192 + (kk + 1) * 128],
                                wq[:, (kq % 2) * 4096 + k * 512:(kq % 2) * 4096 + (k + 1) * 512],
                                start=(kk == 0), stop=(kk == 63)))
                    st[f"wq{kq % 2}"] = em.inc_last("te", "sTE")
                for kq in range(4):
                    em.wait("gp", "sTE", st[f"wq{kq % 2}"])
                    em.put("gp", lambda e, ob=ob, kq=kq: e.dma_start(
                        out=wq[:, (kq % 2) * 4096:(kq % 2 + 1) * 4096],
                        in_=WbTt[ob][:, kq * 4096:(kq + 1) * 4096]).then_inc(SWP[kq % 2], 16))
                    wm = em.inc(f"sW{kq % 2}", 16)
                    em.wait("te", f"sW{kq % 2}", wm)
                    if kq == 0:
                        em.wait("te", "sDV", st["psF"])
                        em.wait("te", "sWC", w_cc)
                    for k in range(8):
                        kk = kq * 8 + k
                        for tl in range(4):
                            em.put("te", lambda e, kk=kk, k=k, kq=kq, tl=tl: e.matmul(
                                ps[4 + tl][:],
                                xnth[:, tl * 4096 + kk * 128:tl * 4096 + (kk + 1) * 128],
                                wq[:, (kq % 2) * 4096 + k * 512:(kq % 2) * 4096 + (k + 1) * 512],
                                start=(kk == 0), stop=False))
                    st[f"wq{kq % 2}"] = em.inc_last("te", "sTE")
                for tl in range(4):
                    em.put("te", lambda e, tl=tl: e.matmul(
                        ps[4 + tl][:], selb(0), bbs[:], start=False, stop=False))
                    em.put("te", lambda e, tl=tl, half=half: e.matmul(
                        ps[4 + tl][:], hwT[:, (half * 4 + tl) * 128:(half * 4 + tl + 1) * 128],
                        blt[:], start=False, stop=True))
                tmc = em.inc_last("te", "sTE")
                st["bbte"] = tmc
                em.wait("sc", "sDV", st["psF"])
                for tl in range(4):
                    r0 = half * 512 + tl * 128
                    em.put("sc", lambda e, r0=r0, ob=ob, tl=tl: e.dma_start(
                        out=xcs[:, tl * 512:(tl + 1) * 512],
                        in_=xo[r0:r0 + 128, ob * 512:(ob + 1) * 512]).then_inc(SX, 16))
                    em.inc("sX", 16)
                r_x = em.cnt["sX"]
                em.wait("dv", "sTE", tmc)
                em.wait("dv", "sO", st.get("out_done", 0))
                for tl in range(4):
                    em.put("dv", lambda e, tl=tl: e.tensor_copy(s_sb[:, tl * 512:(tl + 1) * 512], ps[tl][:]))
                st["psS"] = em.inc_last("dv", "sDV")
                em.wait("ac", "sTE", tmc)
                for tl in range(4):
                    em.put("ac", lambda e, tl=tl: e.activation(gf32[:, tl * 512:(tl + 1) * 512], ps[4 + tl][:], AF.Silu, bias=0.0, scale=1.0))
                acc_m = em.inc_last("ac", "sAC")
                em.wait("dv", "sAC", acc_m)
                em.wait("dv", "sX", r_x)
                for tl in range(4):
                    em.put("dv", lambda e, tl=tl: e.tensor_tensor(s_sb[:, tl * 512:(tl + 1) * 512], s_sb[:, tl * 512:(tl + 1) * 512], gf32[:, tl * 512:(tl + 1) * 512], ALU.mult))
                em.fence("dv")
                for tl in range(4):
                    em.put("dv", lambda e, tl=tl: e.tensor_tensor(s_sb[:, tl * 512:(tl + 1) * 512], s_sb[:, tl * 512:(tl + 1) * 512], xcs[:, tl * 512:(tl + 1) * 512], ALU.add))
                st["psF"] = em.inc_last("dv", "sDV")
                dv_out = st["psF"]
                em.wait("sy", "sDV", dv_out)
                for tl in range(4):
                    r0 = half * 512 + tl * 128
                    em.put("sy", lambda e, r0=r0, ob=ob, tl=tl: e.dma_start(
                        out=out[r0:r0 + 128, ob * 512:(ob + 1) * 512],
                        in_=s_sb[:, tl * 512:(tl + 1) * 512]).then_inc(SO, 16))
                    em.inc("sO", 16)
                st["out_done"] = em.cnt["sO"]
            half_te_done = em.cnt["sTE"]

    def lim(tag):
        return LIMIT is not None and done_tags.append(tag) is None and tag == LIMIT
    done_tags = []
    stop = False
    for w in range(NPW + 1):
        if stop:
            break
        own = (w == NPW)
        with ExitStack() as ph:
            tt = lambda n, sh, dt: ph.enter_context(nc.sbuf_tensor(f"a{w}{n}", sh, dt))
            xa = tt("xa", [128, 4096], F32)
            xq = tt("xq", [128, 2048], F32)
            xnb = tt("xnb", [128, 4096], BF16)
            xnT = tt("xnT", [128, NKC * 1024], BF16)
            wchb = tt("wch", [128, 8192], BF16)
            ext = tt("ext", [128, 1027], F32)
            acc1 = tt("ac1", [128, 1024], F32)
            acc2 = tt("ac2", [128, 1024], F32)
            cvo = tt("cvo", [128, 2048], BF16)
            trT = tt("trT", [128, 1024], BF16)
            mskb = tt("msk", [128, TOWN], F32)
            st["cvo_free"] = [(None, 0), (None, 0)]
            st["acc_free"] = 0
            st["trT_free"] = 0
            phase_barrier()
            phase_a(w, own, xa, xq, xnb, xnT, wchb, ext, acc1, acc2, cvo, trT, mskb)
            flush_block()
        if lim(f"a{w}"):
            stop = True
            continue
        with ExitStack() as ph:
            tt = lambda n, sh, dt: ph.enter_context(nc.sbuf_tensor(f"b{w}{n}", sh, dt))
            hstok = tt("hst", [128, INTER], BF16)
            bcs = tt("bcs", [128, 2048], BF16)
            Btok = tt("btk", [128, 1024], BF16)
            SMb = tt("smb", [128, 1024], BF16)
            dcl = tt("dcl", [128, 512], F32)
            Eb = tt("eb", [128, 512], BF16)
            EAb = tt("eab", [128, 512], F32)
            MTb = tt("mtb", [128, 512], BF16)
            xch = tt("xc2", [128, 256], BF16)
            xds = tt("xd2", [128, 256], BF16)
            msr = tt("msr", [128, 1024], F32)
            rsb = tt("rsb", [128, 1024], BF16)
            g2q = tt("g2q", [128, 1024], BF16)
            xnk = tt("xnk", [128, NKC * 128], BF16)
            RstB = tt("rsB", [128, H * P], BF16)
            ytmp = tt("yt1", [128, 512], F32)
            ytmp2 = tt("yt2", [128, 512], F32)
            ytmp3 = tt("yt3", [128, 512], F32)
            nrt = tt("nrt", [128, 512], F32)
            if own:
                hsf = tt("hsf", [128, 8192], BF16)
                gf = tt("gf", [128, 8192], BF16)
                y_sb = tt("ysb", [128, 8192], BF16)
                st["y_sb"] = y_sb
            else:
                hsf = gf = None
            phase_barrier()
            phase_b(w, own, hstok, hsf, gf, bcs, Btok, SMb, dcl, Eb, EAb, MTb, xch, xds, msr, rsb, g2q, xnk, RstB, ytmp, ytmp2, ytmp3, nrt)
            flush_block()
        if lim(f"b{w}"):
            stop = True
    if stop:
        with ExitStack() as ph:
            for sem in ("sS", "sX", "sT1", "sC0", "sC1", "sO", "sN"):
                em.wait("sy", sem, em.cnt[sem])
            flush_block()
            es.close()
            return nc
    with ExitStack() as ph:
        tt = lambda n, sh, dt: ph.enter_context(nc.sbuf_tensor(f"c{n}", sh, dt))
        nhalf = tt("nh", [128, 64 * 512], BF16)
        xnth = tt("xh", [128, 32 * 512], BF16)
        wq = tt("wq", [128, 8192], BF16)
        s_sb = tt("ss", [128, 2048], F32)
        gf32 = tt("gg", [128, 2048], F32)
        xcs = tt("xc", [128, 2048], F32)
        blt = tt("bl", [128, 512], BF16)
        bbs = tt("bb", [128, 512], BF16)
        phase_barrier()
        phase_c(nhalf, xnth, wq, s_sb, gf32, xcs, blt, bbs)
        total_o = em.cnt["sO"]
        em.put("sy", lambda e, v=total_o: e.wait_ge(SO, v))
        flush_block()

    es.close()
    return nc


def kernel(**inputs):
    f32 = np.float32
    x = np.ascontiguousarray(np.asarray(inputs["L0_attn_x"], f32).reshape(B * S, HID))
    Win = np.asarray(inputs["L0_attn_in_proj_w"], f32)
    conv_w = np.asarray(inputs["L0_attn_conv_w"], f32)[:, 0, :]
    conv_b = np.asarray(inputs["L0_attn_conv_b"], f32)
    dt_bias = np.asarray(inputs["L0_attn_dt_bias"], f32)
    A_log = np.asarray(inputs["L0_attn_A_log"], f32)
    D = np.asarray(inputs["L0_attn_D"], f32)
    norm_w = np.asarray(inputs["L0_attn_norm_w"], f32)
    Wout = np.asarray(inputs["L0_attn_out_proj_w"], f32)
    Wb = np.asarray(inputs["L0_ffn_W_base"], f32)
    b_base = np.asarray(inputs["L0_ffn_b_base"], f32)
    Wg = np.asarray(inputs["L0_ffn_W_gate"], f32)
    Al = np.asarray(inputs["L0_ffn_A"], f32)
    Bl = np.asarray(inputs["L0_ffn_B"], f32)

    WinT = np.ascontiguousarray(
        np.ascontiguousarray(Win.T).reshape(NKC, 128, NCB, 128).transpose(2, 1, 0, 3).reshape(NCB, 128, 4096))
    WoT = np.ascontiguousarray(
        np.ascontiguousarray(Wout.T).reshape(64, 128, 8, 512).transpose(2, 1, 0, 3).reshape(8, 128, 32768))
    WbTt = np.ascontiguousarray(
        np.ascontiguousarray(Wb.T).reshape(32, 128, 8, 512).transpose(2, 1, 0, 3).reshape(8, 128, 16384))
    FW2 = np.ascontiguousarray(
        np.concatenate([np.ascontiguousarray(Al.reshape(128, HID).T),
                        np.ascontiguousarray(Wg.T)], axis=1).reshape(NKC, 128, 136).transpose(1, 0, 2).reshape(128, 4352))
    BlT = np.ascontiguousarray(Bl.transpose(0, 2, 1).reshape(128, 4096))
    cwI = np.ascontiguousarray(conv_w.reshape(80, 128, 4).transpose(1, 0, 2).reshape(128, 320))
    cbI = np.ascontiguousarray(conv_b.reshape(80, 128).T)
    nwI = np.ascontiguousarray(norm_w.reshape(64, 128).T)
    dtbI = np.ascontiguousarray(dt_bias.reshape(128, 1))
    alogI = np.ascontiguousarray(A_log.reshape(128, 1))
    dbcI = np.ascontiguousarray(np.tile(D.reshape(1, 128), (128, 1)))
    bbI = np.ascontiguousarray(np.tile(b_base.reshape(1, 4096), (128, 1)))

    nc = build_nc()
    in_maps = []
    for c in range(NCORE):
        b, q = c // 4, c % 4
        own = x[b * S + q * TOWN: b * S + (q + 1) * TOWN]
        xpc = np.zeros((PREF, HID), f32)
        m = np.zeros(PREF, f32)
        if q > 0:
            xpc[PREF - q * TOWN:] = x[b * S: b * S + q * TOWN]
            m[PREF - q * TOWN:] = 1.0
        mI = np.ascontiguousarray(np.tile(m.reshape(NPW, 1, TOWN), (1, 128, 1)))
        in_maps.append({
            "xo": np.ascontiguousarray(own), "xp": xpc, "mskI": mI,
            "WinT": WinT, "WoT": WoT, "WbTt": WbTt, "FW2": FW2, "BlT": BlT,
            "cwI": cwI, "cbI": cbI, "nwI": nwI, "dtbI": dtbI, "alogI": alogI,
            "dbcI": dbcI, "bbI": bbI,
        })
    res = run_bass_kernel_spmd(nc, in_maps, core_ids=list(range(NCORE)))
    full = np.zeros((B * S, HID), np.float32)
    for c in range(NCORE):
        b, q = c // 4, c % 4
        full[b * S + q * TOWN: b * S + (q + 1) * TOWN] = np.asarray(res.results[c]["out"])
    return full.reshape(B, S, HID)


if __name__ == "__main__":
    import reference
    ins = reference.setup_inputs(0)
    o = kernel(**{k: np.asarray(v) for k, v in ins.items()})
    e = np.asarray(reference.reference(**ins))
    d = (o.astype(np.float64) - e.astype(np.float64))
    print("rvr", (d ** 2).mean() / (e.astype(np.float64) ** 2).mean())
